# version 05 (final)
# baseline (speedup 1.0000x reference)
import contextlib
import math
import numpy as np
import ml_dtypes
import concourse.bass as bass
import concourse.mybir as mybir
from concourse.bass_utils import run_bass_kernel_spmd

F32 = mybir.dt.float32
BF16 = mybir.dt.bfloat16
AF = mybir.ActivationFunctionType
ALU = mybir.AluOpType
AX = mybir.AxisListType
BF = ml_dtypes.bfloat16

CORES = 8
GROUP = 4
DEBUG_TAP = None
STOP_AFTER = None
LN_PARTS = {"s1", "s2", "rows", "rstd", "bc"}
APPLY_MODE = "full"


class Cfg:
    def __init__(self, D, NH, SL, HID, QB, QB2):
        self.E = 128
        self.D = D
        self.DC = D // 128
        self.NH = NH
        assert NH * 128 == D
        self.SL = SL
        self.SEQ = GROUP * SL
        self.HID = HID
        self.HC = HID // 128
        self.HI = self.HC
        self.QB = QB
        self.QB2 = QB2
        self.NQB = SL // QB
        self.NQB2 = SL // QB2
        self.KCH = self.SEQ // 128
        self.QT = QB // 128
        self.SCALE = 1.0 / math.sqrt(self.E)
        self.SPAN = 4
        self.NSPAN = self.HC // self.SPAN
        self.NH2 = QB2 // 512


CFG_REAL = Cfg(D=2048, NH=16, SL=2048, HID=8192, QB=512, QB2=1024)
CFG_SMALL = Cfg(D=512, NH=4, SL=1024, HID=1024, QB=256, QB2=1024)

ENGINES = ("tensor", "vector", "scalar", "gpsimd", "sync")


class Prog:
    def __init__(self):
        self.ops = []
        self.counts = {}

    def n(self, sem):
        return self.counts.get(sem, 0)

    def op(self, engine, fn, waits=(), inc=None):
        waits = [(s, v) for (s, v) in waits if v > 0]
        incs = []
        if inc is not None:
            if isinstance(inc, tuple):
                inc = [inc]
            for s, d in inc:
                self.counts[s] = self.counts.get(s, 0) + d
                incs.append((s, d))
        self.ops.append((engine, waits, incs, fn))
        return self.counts[incs[0][0]] if incs else None

    def inc_last(self, engine, inc):
        for i in range(len(self.ops) - 1, -1, -1):
            e, waits, incs, fn = self.ops[i]
            if e == engine:
                s, d = inc
                self.counts[s] = self.counts.get(s, 0) + d
                incs.append((s, d))
                return self.counts[s]
        raise RuntimeError(f"no prior op on {engine}")

    def check_deadlock(self):
        idx = {e: [] for e in ENGINES}
        for i, (e, _, _, _) in enumerate(self.ops):
            idx[e].append(i)
        pos = {e: 0 for e in ENGINES}
        counts = {}
        progressed = True
        while progressed:
            progressed = False
            for e in ENGINES:
                while pos[e] < len(idx[e]):
                    i = idx[e][pos[e]]
                    _, waits, incs, _ = self.ops[i]
                    if all(counts.get(s, 0) >= v for s, v in waits):
                        for s, d in incs:
                            counts[s] = counts.get(s, 0) + d
                        pos[e] += 1
                        progressed = True
                    else:
                        break
        stuck = [(e, pos[e], len(idx[e])) for e in ENGINES if pos[e] < len(idx[e])]
        if stuck:
            lines = []
            for e, p, tot in stuck:
                i = idx[e][p]
                _, waits, _, _ = self.ops[i]
                unmet = [(s, v, counts.get(s, 0)) for s, v in waits
                         if counts.get(s, 0) < v]
                lines.append(f"engine {e} stuck at op {p}/{tot} waiting {unmet}")
            raise RuntimeError("plan deadlock:\n" + "\n".join(lines))

    def emit(self, block, sems):
        by_engine = {e: [] for e in ENGINES}
        for e, waits, incs, fn in self.ops:
            by_engine[e].append((waits, incs, fn))

        def body(eng, elist):
            waited = {}
            for waits, incs, fn in elist:
                for s, v in waits:
                    if waited.get(s, -1) < v:
                        eng.wait_ge(sems[s], v)
                        waited[s] = v
                inst = fn(eng)
                for s, d in incs:
                    if inst is None:
                        inst = eng.nop()
                    inst.then_inc(sems[s], d)

        block.tensor(lambda eng: body(eng, by_engine["tensor"]))
        block.vector(lambda eng: body(eng, by_engine["vector"]))
        block.scalar(lambda eng: body(eng, by_engine["scalar"]))
        block.gpsimd(lambda eng: body(eng, by_engine["gpsimd"]))
        block.sync(lambda eng: body(eng, by_engine["sync"]))


def build_program(cfg: Cfg):
    c = cfg
    SL, D, DC, NH, QB, QB2, HC, HID = (c.SL, c.D, c.DC, c.NH, c.QB, c.QB2,
                                       c.HC, c.HID)
    nc = bass.Bass("TRN2", target_bir_lowering=False,
                   detect_race_conditions=False)

    x_in = nc.dram_tensor("x_in", [SL, D], F32, kind="ExternalInput").ap()
    wqkv = nc.dram_tensor("wqkv", [D, 3 * D], F32, kind="ExternalInput").ap()
    bqkv = nc.dram_tensor("bqkv", [1, 3 * D], F32, kind="ExternalInput").ap()
    wout = nc.dram_tensor("wout", [D, D], F32, kind="ExternalInput").ap()
    bout = nc.dram_tensor("bout", [1, D], F32, kind="ExternalInput").ap()
    w1 = nc.dram_tensor("w1", [HID, D], F32, kind="ExternalInput").ap()
    w2 = nc.dram_tensor("w2", [D, HID], F32, kind="ExternalInput").ap()
    w3 = nc.dram_tensor("w3", [HID, D], F32, kind="ExternalInput").ap()
    Hi_in = nc.dram_tensor("Hi_in", [c.HI, c.HI], BF16, kind="ExternalInput").ap()
    H128_in = nc.dram_tensor("H128_in", [128, 128], BF16, kind="ExternalInput").ap()
    ident_in = nc.dram_tensor("ident_in", [128, 128], F32, kind="ExternalInput").ap()
    out_hbm = nc.dram_tensor("out", [SL, D], F32, kind="ExternalOutput").ap()

    xbf = nc.dram_tensor("xbf", [SL, D], BF16).ap()
    q_scr = nc.dram_tensor("q_scr", [D, SL], BF16).ap()
    k_in = nc.dram_tensor("k_in", [D, SL], BF16).ap()
    v_in = nc.dram_tensor("v_in", [SL, D], BF16).ap()
    k_out = nc.dram_tensor("k_out", [GROUP * D, SL], BF16).ap()
    v_out = nc.dram_tensor("v_out", [GROUP * SL, D], BF16).ap()
    st_in = nc.dram_tensor("st_in", [1, 16], F32).ap()
    st_out = nc.dram_tensor("st_out", [GROUP, 16], F32).ap()
    a_scr = nc.dram_tensor("a_scr", [D, SL], BF16).ap()
    x2_scr = nc.dram_tensor("x2_scr", [D, SL], BF16).ap()
    w1n = nc.dram_tensor("w1n", [HID, D], BF16).ap()
    w3n = nc.dram_tensor("w3n", [HID, D], BF16).ap()
    w1t = nc.dram_tensor("w1t", [D, HID], BF16).ap()
    w3t = nc.dram_tensor("w3t", [D, HID], BF16).ap()
    s1_scr = nc.dram_tensor("s1_scr", [HC, D, 128], BF16).ap()
    w2t = nc.dram_tensor("w2t", [HID, D], BF16).ap()
    BQR = max(3 * NH, 16)
    BOR = max(DC, 16)
    bqkv_bf = nc.dram_tensor("bqkv_bf", [1, BQR * 128], BF16).ap()
    bout_bf = nc.dram_tensor("bout_bf", [1, BOR * 128], BF16).ap()

    R1 = nc.alloc_sbuf_tensor("R1", [128, max(DC * SL, HC * QB)], BF16)
    R2sz = max(2 * DC * 512 + HID + 64 * 128,
               4 * c.SEQ,
               NH * D,
               4 * DC * c.SPAN * 128)
    R2 = nc.alloc_sbuf_tensor("R2", [128, R2sz], BF16)
    PW = 2048
    R3_PREP = 2 * SL + 4 * QB2 + QB2 + 2 * QB2
    R3sz = max(R3_PREP + max(2 * min(PW, max(HID, D)), 5 * 768),
               2 * DC * QB, DC * QB)
    R3 = nc.alloc_sbuf_tensor("R3", [128, R3sz], BF16)
    w2s = nc.alloc_sbuf_tensor("w2s", [128, 4 * 768], BF16)
    rs32 = nc.alloc_sbuf_tensor("rs32", [128, QB2], F32)
    rrs_bc = nc.alloc_sbuf_tensor("rrs_bc", [128, QB2], BF16)
    bc_rstd = nc.alloc_sbuf_tensor("bc_rstd", [128, QB], BF16)
    bc_nu = nc.alloc_sbuf_tensor("bc_nu", [128, QB], BF16)
    rows = nc.alloc_sbuf_tensor("rows", [1, max(4 * QB, QB2)], F32)
    rowbf = nc.alloc_sbuf_tensor("rowbf", [1, max(QB, QB2)], BF16)
    biasT = nc.alloc_sbuf_tensor("biasT", [128, 1], F32)
    STG = max(4 * QB, 2048)
    stage = nc.alloc_sbuf_tensor("stage", [128, STG], BF16)
    stage2 = nc.alloc_sbuf_tensor("stage2", [128, 2 * QB], BF16)
    sg_st = nc.alloc_sbuf_tensor("sg_st", [128, 2 * QB], BF16)
    hid_st = nc.alloc_sbuf_tensor("hid_st", [128, 2 * QB], BF16)
    o32 = nc.alloc_sbuf_tensor("o32", [128, 2 * QB], F32)
    tr_st = nc.alloc_sbuf_tensor("tr_st", [128, 2 * 128], F32)
    c_ones = nc.alloc_sbuf_tensor("c_ones", [1, max(QB, 512)], BF16)
    c_onecol = nc.alloc_sbuf_tensor("c_onecol", [128, 1], BF16)
    c_onecol32 = nc.alloc_sbuf_tensor("c_onecol32", [128, 1], F32)
    c_Hi = nc.alloc_sbuf_tensor("c_Hi", [c.HI, c.HI], BF16)
    c_H128 = nc.alloc_sbuf_tensor("c_H128", [128, 128], BF16)
    c_ident = nc.alloc_sbuf_tensor("c_ident", [128, 128], F32)
    c_misc = nc.alloc_sbuf_tensor("c_misc", [1, 32], F32)
    c_bq2 = nc.alloc_sbuf_tensor("c_bq2", [128, max(3 * NH, 16)], BF16)
    c_bo2 = nc.alloc_sbuf_tensor("c_bo2", [128, max(DC, 16)], BF16)
    c_bqv = nc.alloc_sbuf_tensor("c_bqv", [1, D], BF16)

    R2_WA = 0
    R2_PREP = 2 * DC * 512
    R2_SB = R2_PREP + HID
    R2_K = 0
    R2_V = 2 * c.SEQ
    R3_Q = 0
    R3_P = 2 * SL
    R3_PS = R3_P + 4 * QB2
    R3_AS = R3_PS + QB2
    R3_A = 0
    R3_O = DC * QB
    R3_LN3 = 0
    R3_X2 = DC * QB

    pA = nc.alloc_psum_tensor("pA", [128, 1024], F32)
    pB = nc.alloc_psum_tensor("pB", [128, 1024], F32)
    pC = nc.alloc_psum_tensor("pC", [128, 1024], F32)
    pD = nc.alloc_psum_tensor("pD", [128, 512], F32)
    pE = nc.alloc_psum_tensor("pE", [128, 512], F32)
    pb = [pA[:, 0:512], pA[:, 512:1024], pB[:, 0:512], pB[:, 512:1024],
          pC[:, 0:512], pC[:, 512:1024], pD[:, :], pE[:, :]]

    RW = QB
    ROW_S1, ROW_S2, ROW_RSTD, ROW_NU = 0, RW, 2 * RW, 3 * RW

    sem_names = (["g_x", "g_b", "g_wa0", "g_wa1", "g_sb", "g_sbs", "g_w1",
                  "g_w3", "g_kv0", "g_kv1", "g_as", "g_as0", "g_as1",
                  "g_q0", "g_q1", "g_q2", "g_q3", "g_qkv", "g_sst", "g_x2s",
                  "g_x2l", "g_al", "g_wo", "g_st", "g_p0", "g_p1",
                  "s_c", "s_x", "s_p0", "s_p1", "s_sp0", "s_sp1",
                  "s_w2s0", "s_w2s1", "s_w2s2", "s_w2s3", "s_w2s4",
                  "o_tr0", "o_tr1",
                  "pe", "ve", "se", "cc", "outd", "sqf"] +
                 [f"bk{i}" for i in range(8)])

    P = Prog()

    def T(fn, waits=(), inc=None):
        return P.op("tensor", fn, waits, inc)

    def V(fn, waits=(), inc=None):
        return P.op("vector", fn, waits, inc)

    def S(fn, waits=(), inc=None):
        return P.op("scalar", fn, waits, inc)

    def G(fn, waits=(), inc=None):
        return P.op("gpsimd", fn, waits, inc)

    def Y(fn, waits=(), inc=None):
        return P.op("sync", fn, waits, inc)

    bank_prod = [0] * 8
    bank_free = [None] * 8

    def bw(i):
        return [bank_free[i]] if bank_free[i] else []

    def br(i):
        return [(f"bk{i}", bank_prod[i])]

    def mp(i):
        bank_prod[i] += 1
        return (f"bk{i}", 1)

    def freed(i, sem):
        bank_free[i] = (sem, P.n(sem))

    Y(lambda e: e.dma_start(out=c_Hi[:, :], in_=Hi_in), inc=("s_c", 16))
    Y(lambda e: e.dma_start(out=c_H128[:, :], in_=H128_in), inc=("s_c", 16))
    Y(lambda e: e.dma_start(out=c_ident[:, :], in_=ident_in), inc=("s_c", 16))
    consts_ld = P.n("s_c")

    V(lambda e: e.memset(c_ones[:, :], 1.0))
    V(lambda e: e.memset(c_onecol[:, :], 1.0))
    V(lambda e: e.memset(c_onecol32[:, :], 1.0))
    V(lambda e: e.memset(c_misc[:, :], 0.0))
    V(lambda e: e.memset(c_misc[0:1, 0:1], 1e-5), inc=("ve", 1))
    consts_ve = P.n("ve")

    NXC = max(1, (SL * D * 4) // (16 << 20))
    for i in range(NXC):
        rs = SL // NXC
        G(lambda e, i=i, rs=rs: e.dma_start(out=xbf[i * rs:(i + 1) * rs, :],
                                            in_=x_in[i * rs:(i + 1) * rs, :]),
          inc=("g_x", 16))
    G(lambda e: e.dma_start(out=bqkv_bf[0:1, 0:3 * D], in_=bqkv), inc=("g_b", 16))
    G(lambda e: e.dma_start(out=bout_bf[0:1, 0:D], in_=bout), inc=("g_b", 16))
    Y(lambda e: e.dma_start(out=c_bq2[:, 0:BQR],
                            in_=bass.AP(bqkv_bf.tensor, 0, [[128, BQR], [1, 128]]),
                            transpose=True),
      waits=[("g_b", 32)], inc=("s_c", 16))
    Y(lambda e: e.dma_start(out=c_bo2[:, 0:BOR],
                            in_=bass.AP(bout_bf.tensor, 0, [[128, BOR], [1, 128]]),
                            transpose=True),
      waits=[("g_b", 32)], inc=("s_c", 16))
    G(lambda e: e.dma_start(out=c_bqv[:, :],
                            in_=bqkv_bf[0:1, 2 * D:3 * D]),
      waits=[("g_b", 32)], inc=("g_b", 16))
    bias_rows = P.n("g_b")
    bias_cols = P.n("s_c")

    NDB = D // 64
    sb_store = 0
    for db in range(NDB):
        def fld(e, db=db):
            src = bass.AP(w2.tensor, db * 64 * HID,
                          [[128, c.HI], [HID, 64], [1, 128]])
            dst = R2[:c.HI, R2_SB:R2_SB + 64 * 128].rearrange(
                "p (d j) -> p d j", d=64)
            return e.dma_start(out=dst, in_=src)
        G(fld, waits=[("g_sbs", sb_store)], inc=("g_sb", 16))
        sbld = P.n("g_sb")
        nmm = (64 * 128) // 512
        for g in range(nmm):
            def fmm(e, g=g):
                rhs = R2[:c.HI, R2_SB + g * 512:R2_SB + (g + 1) * 512]
                return e.matmul(pb[7][0:c.HI, 0:512], c_Hi[:, :], rhs,
                                start=True, stop=True)
            T(fmm, waits=[("g_sb", sbld), ("s_c", consts_ld)] + bw(7), inc=mp(7))
            guard = [("g_sbs", sb_store)] if g % 4 == 0 and g > 0 else []
            def fev(e, g=g):
                return e.tensor_copy(
                    stage[0:c.HI, (g % 4) * 512:(g % 4) * 512 + 512],
                    pb[7][0:c.HI, 0:512])
            V(fev, waits=br(7) + guard, inc=("ve", 1))
            freed(7, "ve")
            ev_ve = P.n("ve")
            if g % 4 == 3:
                def fst(e, g=g, db=db):
                    d0 = db * 64 + (g - 3) * 4
                    dst = bass.AP(s1_scr.tensor, d0 * 128,
                                  [[D * 128, c.HI], [128, 16], [1, 128]])
                    src = stage[0:c.HI, 0:2048].rearrange(
                        "p (d j) -> p d j", d=16)
                    return e.dma_start(out=dst, in_=src)
                G(fst, waits=[("ve", ev_ve)], inc=("g_sbs", 16))
                sb_store = P.n("g_sbs")
    s1_done = sb_store

    NWC = max(1, (HID * D * 4) // (16 << 20))
    for i in range(NWC):
        rs = HID // NWC
        G(lambda e, i=i, rs=rs: e.dma_start(out=w1n[i * rs:(i + 1) * rs, :],
                                            in_=w1[i * rs:(i + 1) * rs, :]),
          inc=("g_w1", 16))
    w1n_done = P.n("g_w1")
    for i in range(NWC):
        rs = HID // NWC
        G(lambda e, i=i, rs=rs: e.dma_start(out=w3n[i * rs:(i + 1) * rs, :],
                                            in_=w3[i * rs:(i + 1) * rs, :]),
          inc=("g_w3", 16))
    w3n_done = P.n("g_w3")

    for dc in range(DC):
        def f(e, dc=dc):
            return e.dma_start(out=R1[:, dc * SL:(dc + 1) * SL],
                               in_=xbf[:, dc * 128:(dc + 1) * 128],
                               transpose=True)
        Y(f, waits=[("g_x", 16 * NXC)], inc=("s_x", 16))
    x_ld = P.n("s_x")

    def emit_ln_stats(src_tile, waits_data, stat_banks, bcast_bank, tag=""):
        if LN_PARTS != {"s1", "s2", "rows", "rstd", "bc"}:
            return _emit_ln_stats_dbg(src_tile, waits_data, stat_banks,
                                      bcast_bank)
        """LN over partitions (D features). Emits stats, rstd/nu rows and the
        broadcast tiles bc_rstd/bc_nu [128 x QB]. Returns ve mark."""
        b1, b2 = stat_banks
        for dc in range(DC):
            def f1(e, dc=dc):
                return e.matmul(pb[b1][0:1, 0:QB], c_onecol[:, :], src_tile(dc),
                                start=(dc == 0), stop=(dc == DC - 1))
            T(f1, waits=waits_data(dc) + (bw(b1) if dc == 0 else []),
              inc=(mp(b1) if dc == DC - 1 else None))
        sq_base = P.n("sqf")
        sqv = []
        for dc in range(DC):
            g = [("sqf", sq_base + dc - 1)] if dc >= 2 else []
            def fsq(e, dc=dc):
                t = src_tile(dc)
                return e.tensor_tensor(
                    stage2[:, (dc % 2) * QB:(dc % 2) * QB + QB], t, t, ALU.mult)
            V(fsq, waits=waits_data(dc) + g, inc=("ve", 1))
            sqv.append(P.n("ve"))
        for dc in range(DC):
            def f2(e, dc=dc):
                return e.matmul(pb[b2][0:1, 0:QB], c_onecol[:, :],
                                stage2[:, (dc % 2) * QB:(dc % 2) * QB + QB],
                                start=(dc == 0), stop=(dc == DC - 1))
            T(f2, waits=[("ve", sqv[dc])] + (bw(b2) if dc == 0 else []),
              inc=("sqf", 1))
        V(lambda e: e.tensor_scalar_mul(rows[0:1, ROW_S1:ROW_S1 + QB],
                                        pb[b1][0:1, 0:QB], 1.0 / D),
          waits=br(b1), inc=("ve", 1))
        freed(b1, "ve")
        V(lambda e: e.tensor_scalar_mul(rows[0:1, ROW_S2:ROW_S2 + QB],
                                        pb[b2][0:1, 0:QB], 1.0 / D),
          waits=[("sqf", sq_base + DC)], inc=("ve", 1))
        freed(b2, "ve")
        V(lambda e: e.tensor_tensor(rows[0:1, ROW_NU:ROW_NU + QB],
                                    rows[0:1, ROW_S1:ROW_S1 + QB],
                                    rows[0:1, ROW_S1:ROW_S1 + QB], ALU.mult))
        V(lambda e: e.tensor_sub(rows[0:1, ROW_S2:ROW_S2 + QB],
                                 rows[0:1, ROW_S2:ROW_S2 + QB],
                                 rows[0:1, ROW_NU:ROW_NU + QB]), inc=("ve", 1))
        var_ve = P.n("ve")
        if DEBUG_TAP:
            V(lambda e, tag=tag: DEBUG_TAP(e, f"lnvar_{tag}"),
              waits=[("ve", var_ve)])
        S(lambda e: e.activation(out=rows[0:1, ROW_RSTD:ROW_RSTD + QB],
                                 in_=rows[0:1, ROW_S2:ROW_S2 + QB],
                                 func=AF.Ln, bias=c_misc[0:1, 0:1], scale=1.0),
          waits=[("ve", var_ve), ("ve", consts_ve)])
        S(lambda e: e.activation(out=rows[0:1, ROW_RSTD:ROW_RSTD + QB],
                                 in_=rows[0:1, ROW_RSTD:ROW_RSTD + QB],
                                 func=AF.Exp, scale=-0.5), inc=("se", 1))
        rstd_se = P.n("se")
        V(lambda e: e.tensor_tensor(rows[0:1, ROW_NU:ROW_NU + QB],
                                    rows[0:1, ROW_S1:ROW_S1 + QB],
                                    rows[0:1, ROW_RSTD:ROW_RSTD + QB], ALU.mult),
          waits=[("se", rstd_se)])
        V(lambda e: e.tensor_copy(rowbf[0:1, 0:QB],
                                  rows[0:1, ROW_RSTD:ROW_RSTD + QB]),
          inc=("ve", 1))
        rbf_ve = P.n("ve")
        for half in range((QB + 511) // 512):
            w = min(512, QB - half * 512)
            def fb1(e, half=half, w=w):
                return e.matmul(pb[bcast_bank][:, 0:w], c_ones[0:1, 0:128],
                                rowbf[0:1, half * 512:half * 512 + w],
                                start=True, stop=True)
            T(fb1, waits=[("ve", rbf_ve)] + bw(bcast_bank), inc=mp(bcast_bank))
            def fb2(e, half=half, w=w):
                return e.tensor_copy(bc_rstd[:, half * 512:half * 512 + w],
                                     pb[bcast_bank][:, 0:w])
            V(fb2, waits=br(bcast_bank), inc=("ve", 1))
            freed(bcast_bank, "ve")
        V(lambda e: e.tensor_copy(rowbf[0:1, 0:QB],
                                  rows[0:1, ROW_NU:ROW_NU + QB]), inc=("ve", 1))
        nbf_ve = P.n("ve")
        for half in range((QB + 511) // 512):
            w = min(512, QB - half * 512)
            def fb3(e, half=half, w=w):
                return e.matmul(pb[bcast_bank][:, 0:w], c_ones[0:1, 0:128],
                                rowbf[0:1, half * 512:half * 512 + w],
                                start=True, stop=True)
            T(fb3, waits=[("ve", nbf_ve)] + bw(bcast_bank), inc=mp(bcast_bank))
            def fb4(e, half=half, w=w):
                return e.tensor_copy(bc_nu[:, half * 512:half * 512 + w],
                                     pb[bcast_bank][:, 0:w])
            V(fb4, waits=br(bcast_bank), inc=("ve", 1))
            freed(bcast_bank, "ve")
        return P.n("ve")

    def _emit_ln_stats_dbg(src_tile, waits_data, stat_banks, bcast_bank):
        b1, b2 = stat_banks
        if "s1" in LN_PARTS:
            for dc in range(DC):
                def f1(e, dc=dc):
                    return e.matmul(pb[b1][0:1, 0:QB], c_onecol[:, :],
                                    src_tile(dc),
                                    start=(dc == 0), stop=(dc == DC - 1))
                T(f1, waits=waits_data(dc) + (bw(b1) if dc == 0 else []),
                  inc=(mp(b1) if dc == DC - 1 else None))
            V(lambda e: e.tensor_scalar_mul(rows[0:1, ROW_S1:ROW_S1 + QB],
                                            pb[b1][0:1, 0:QB], 1.0 / D),
              waits=br(b1), inc=("ve", 1))
            freed(b1, "ve")
        if "s2" in LN_PARTS:
            sq_base = P.n("sqf")
            sqv = []
            for dc in range(DC):
                g = [("sqf", sq_base + dc - 1)] if dc >= 2 else []
                def fsq(e, dc=dc):
                    t = src_tile(dc)
                    return e.tensor_tensor(
                        stage2[:, (dc % 2) * QB:(dc % 2) * QB + QB], t, t,
                        ALU.mult)
                V(fsq, waits=waits_data(dc) + g, inc=("ve", 1))
                sqv.append(P.n("ve"))
            for dc in range(DC):
                def f2(e, dc=dc):
                    return e.matmul(pb[b2][0:1, 0:QB], c_onecol[:, :],
                                    stage2[:, (dc % 2) * QB:(dc % 2) * QB + QB],
                                    start=(dc == 0), stop=(dc == DC - 1))
                T(f2, waits=[("ve", sqv[dc])] + (bw(b2) if dc == 0 else []),
                  inc=("sqf", 1))
            V(lambda e: e.tensor_scalar_mul(rows[0:1, ROW_S2:ROW_S2 + QB],
                                            pb[b2][0:1, 0:QB], 1.0 / D),
              waits=[("sqf", sq_base + DC)], inc=("ve", 1))
            freed(b2, "ve")
        V(lambda e: e.memset(bc_rstd[:, 0:QB], 1.0))
        V(lambda e: e.memset(bc_nu[:, 0:QB], 0.0), inc=("ve", 1))
        return P.n("ve")

    NOCG = (3 * D) // 512
    wa_free = [None, None]
    prev_qb_guard = []
    stg_free = [0, 0, 0, 0]
    stg_free_ve = [0, 0, 0, 0]
    qkv_bank = 0

    for qb in range(0 if STOP_AFTER == "A0" else c.NQB):
        def xsrc(dc, qb=qb):
            return R1[:, dc * SL + qb * QB: dc * SL + qb * QB + QB]
        bc_ve = emit_ln_stats(xsrc, lambda dc: [("s_x", x_ld)], (4, 5), 6,
                              tag=f"A{qb}")
        for dc in range(DC):
            def fl(e, dc=dc, xsrc=xsrc):
                return e.tensor_tensor(R3[:, dc * QB:(dc + 1) * QB],
                                       xsrc(dc), bc_rstd[:, 0:QB], ALU.mult)
            V(fl, waits=[("ve", bc_ve)] + prev_qb_guard)
            def fl2(e, dc=dc, qb=qb):
                return e.tensor_sub(R3[:, dc * QB:(dc + 1) * QB],
                                    R3[:, dc * QB:(dc + 1) * QB],
                                    bc_nu[:, 0:QB])
            V(fl2, inc=("ve", 1))
        ln1_all = P.n("ve")
        if DEBUG_TAP:
            T(lambda e, qb=qb: DEBUG_TAP(e, f"qkv_start_qb{qb}"),
              waits=[("ve", ln1_all)])

        for ocg in range(0 if STOP_AFTER == "A1a" else NOCG):
            if STOP_AFTER == "A1b" and ocg * 512 >= 2 * D:
                continue
            if STOP_AFTER == "A1c" and ocg * 512 >= D:
                continue
            slot = ocg % 2
            def fwa(e, ocg=ocg, slot=slot):
                src = bass.AP(wqkv.tensor, ocg * 512,
                              [[3 * D, 128], [128 * 3 * D, DC], [1, 512]])
                dst = R2[:, R2_WA + slot * DC * 512:
                         R2_WA + slot * DC * 512 + DC * 512] \
                    .rearrange("p (dc k) -> p dc k", dc=DC)
                return e.dma_start(out=dst, in_=src)
            G(fwa, waits=([wa_free[slot]] if wa_free[slot] else []),
              inc=(f"g_wa{slot}", 16))
            ready = (f"g_wa{slot}", P.n(f"g_wa{slot}"))
            is_v = (ocg * 512 >= 2 * D)
            for j in range(4):
                oc = ocg * 4 + j
                stslot = oc % 4
                if not is_v:
                    bk = qkv_bank % 3
                    qkv_bank += 1
                    for dc in range(DC):
                        def fm(e, dc=dc, bk=bk, j=j, slot=slot, qb=qb):
                            lhs = R2[:, R2_WA + slot * DC * 512 + dc * 512 +
                                     j * 128:
                                     R2_WA + slot * DC * 512 + dc * 512 +
                                     (j + 1) * 128]
                            return e.matmul(pb[bk][:, 0:QB], lhs,
                                            R3[:, dc * QB:(dc + 1) * QB],
                                            start=(dc == 0),
                                            stop=(dc == DC - 1))
                        T(fm, waits=[("ve", ln1_all), ready] +
                          (bw(bk) if dc == 0 else []),
                          inc=(mp(bk) if dc == DC - 1 else None))
                    def fe(e, bk=bk, stslot=stslot, oc=oc):
                        return e.add(
                            out=stage[:, stslot * QB:(stslot + 1) * QB],
                            in_=pb[bk][:, 0:QB], add=c_bq2[:, oc:oc + 1])
                    S(fe, waits=br(bk) + [(f"g_q{stslot}", stg_free[stslot]),
                                          ("ve", stg_free_ve[stslot]),
                                          ("s_c", bias_cols)],
                      inc=("se", 1))
                    freed(bk, "se")
                    ev_se = P.n("se")
                    def fsq(e, stslot=stslot):
                        t = stage[:, stslot * QB:(stslot + 1) * QB]
                        return e.tensor_tensor(stage2[:, 0:QB], t, t, ALU.mult)
                    V(fsq, waits=[("se", ev_se), (f"bk4", bank_prod[4])],
                      inc=("ve", 1))
                    sq_ve = P.n("ve")
                    def fnm(e):
                        return e.matmul(pb[4][0:1, 0:QB], c_onecol[:, :],
                                        stage2[:, 0:QB], start=True, stop=True)
                    T(fnm, waits=[("ve", sq_ve)] + bw(4), inc=mp(4))
                    midx = 1 if oc < NH else 2
                    V(lambda e: e.reduce_max(rows[0:1, ROW_S1:ROW_S1 + 1],
                                             pb[4][0:1, 0:QB], axis=AX.X),
                      waits=br(4), inc=("ve", 1))
                    freed(4, "ve")
                    def fmx2(e, midx=midx):
                        return e.tensor_max(c_misc[0:1, midx:midx + 1],
                                            c_misc[0:1, midx:midx + 1],
                                            rows[0:1, ROW_S1:ROW_S1 + 1])
                    V(fmx2, inc=("ve", 1))
                    stg_free_ve[stslot] = sq_ve
                    if oc < NH:
                        def fsQ(e, oc=oc, qb=qb, stslot=stslot):
                            return e.dma_start(
                                out=q_scr[oc * 128:(oc + 1) * 128,
                                          qb * QB:(qb + 1) * QB],
                                in_=stage[:, stslot * QB:(stslot + 1) * QB])
                    else:
                        def fsQ(e, oc=oc, qb=qb, stslot=stslot):
                            return e.dma_start(
                                out=k_in[(oc - NH) * 128:(oc - NH + 1) * 128,
                                         qb * QB:(qb + 1) * QB],
                                in_=stage[:, stslot * QB:(stslot + 1) * QB])
                    G(fsQ, waits=[("se", ev_se)], inc=(f"g_q{stslot}", 16))
                    stg_free[stslot] = P.n(f"g_q{stslot}")
                else:
                    voc = oc - 2 * NH
                    for qt in range(c.QT):
                        for dc in range(DC):
                            def fv(e, dc=dc, qt=qt, j=j, slot=slot, qb=qb):
                                lhs = R3[:, dc * QB + qt * 128:
                                     dc * QB + (qt + 1) * 128]
                                rhs = R2[:, R2_WA + slot * DC * 512 +
                                         dc * 512 + j * 128:
                                         R2_WA + slot * DC * 512 +
                                         dc * 512 + (j + 1) * 128]
                                return e.matmul(
                                    pb[3][:, qt * 128:(qt + 1) * 128],
                                    lhs, rhs, start=(dc == 0), stop=False)
                            T(fv, waits=[("ve", ln1_all), ready] +
                              (bw(3) if (dc == 0 and qt == 0) else []))
                        def fvb(e, voc=voc, qt=qt):
                            return e.matmul(pb[3][:, qt * 128:(qt + 1) * 128],
                                            c_ones[0:1, 0:128],
                                            c_bqv[0:1, voc * 128:(voc + 1) * 128],
                                            start=False, stop=True)
                        T(fvb, waits=[("g_b", bias_rows)],
                          inc=(mp(3) if qt == c.QT - 1 else None))
                    def fve(e, stslot=stslot):
                        return e.copy(out=stage[:, stslot * QB:(stslot + 1) * QB],
                                      in_=pb[3][:, 0:QB])
                    S(fve, waits=br(3) + [(f"g_q{stslot}", stg_free[stslot]),
                                          ("ve", stg_free_ve[stslot])],
                      inc=("se", 1))
                    freed(3, "se")
                    vev_se = P.n("se")
                    def fvs(e, voc=voc, qb=qb, stslot=stslot):
                        dst = bass.AP(v_in.tensor, (qb * QB) * D + voc * 128,
                                      [[D, 128], [128 * D, c.QT], [1, 128]])
                        src = stage[:, stslot * QB:(stslot + 1) * QB] \
                            .rearrange("p (t e) -> p t e", t=c.QT)
                        return e.dma_start(out=dst, in_=src)
                    G(fvs, waits=[("se", vev_se)], inc=(f"g_q{stslot}", 16))
                    stg_free[stslot] = P.n(f"g_q{stslot}")
                    stg_free_ve[stslot] = P.n("ve")
            wa_free[slot] = br(3 if is_v else 4)[0]
        prev_qb_guard = [w for w in wa_free if w]
    qkv_stores = [(f"g_q{i}", P.n(f"g_q{i}")) for i in range(4)]
    normmax_ve = P.n("ve")

    G(lambda e: e.dma_start(out=st_in, in_=c_misc[0:1, 0:16]),
      waits=[("ve", normmax_ve)], inc=("g_sst", 16))

    PREP_W = min(PW, max(HID, D))
    def emit_prep_pair(src_fn, dst_fn, slot, extra_wait, w):
        off = R3_PREP + slot * PREP_W
        def ftr(e, off=off):
            return e.dma_start(out=R3[:, off:off + w],
                               in_=src_fn(), transpose=True)
        Y(ftr, waits=extra_wait + [(f"g_p{slot}", P.n(f"g_p{slot}"))],
          inc=(f"s_p{slot}", 16))
        ld = P.n(f"s_p{slot}")
        def fst(e, off=off):
            return e.dma_start(out=dst_fn(), in_=R3[:, off:off + w])
        G(fst, waits=[(f"s_p{slot}", ld)], inc=(f"g_p{slot}", 16))

    nslots = 2

    SKIP_TAIL = STOP_AFTER in ("A0", "A1", "A1a", "A1b", "A1c")
    si = 0
    PW1 = min(PW, HID)
    for (nat, tdst, wgate) in (() if SKIP_TAIL else
                               ((w1n, w1t, "g_w1"), (w3n, w3t, "g_w3"))):
        for dc in range(DC):
            for qq in range(HID // PW1):
                slot = si % nslots
                si += 1
                emit_prep_pair(
                    (lambda nat=nat, dc=dc, qq=qq:
                     nat[qq * PW1:(qq + 1) * PW1, dc * 128:(dc + 1) * 128]),
                    (lambda tdst=tdst, dc=dc, qq=qq:
                     tdst[dc * 128:(dc + 1) * 128, qq * PW1:(qq + 1) * PW1]),
                    slot, [(wgate, 16 * NWC)], PW1)
    PW2 = min(PW, D)
    for hc in range(0 if SKIP_TAIL else HC):
        for qq in range(D // PW2):
            slot = si % nslots
            si += 1
            def srcf(hc=hc, qq=qq):
                return bass.AP(s1_scr.tensor, hc * D * 128 + qq * PW2 * 128,
                               [[128, PW2], [1, 128]])
            def dstf(hc=hc, qq=qq):
                return w2t[hc * 128:(hc + 1) * 128, qq * PW2:(qq + 1) * PW2]
            emit_prep_pair(srcf, dstf, slot, [("g_sbs", s1_done)], PW2)
    preps_done = [("g_p0", P.n("g_p0")), ("g_p1", P.n("g_p1"))]

    groups = [[0, 1, 2, 3], [4, 5, 6, 7]]
    if SKIP_TAIL:
        P.check_deadlock()
        with contextlib.ExitStack() as ctx:
            sems = {s: ctx.enter_context(nc.semaphore(s)) for s in sem_names}
            block = ctx.enter_context(nc.Block())
            P.emit(block, sems)
        return nc
    G(lambda e: e.collective_compute(
        "AllGather", ALU.bypass, replica_groups=groups,
        ins=[k_in.opt()], outs=[k_out.opt()]),
      waits=qkv_stores, inc=("cc", 1))
    G(lambda e: e.collective_compute(
        "AllGather", ALU.bypass, replica_groups=groups,
        ins=[v_in.opt()], outs=[v_out.opt()]), inc=("cc", 1))
    G(lambda e: e.collective_compute(
        "AllGather", ALU.bypass, replica_groups=groups,
        ins=[st_in.opt()], outs=[st_out.opt()]),
      waits=[("g_sst", 16)], inc=("cc", 1))
    cc_done = P.n("cc")

    G(lambda e: e.dma_start(out=rows[0:1, 0:GROUP * 16],
                            in_=bass.AP(st_out.tensor, 0, [[1, 1], [1, GROUP * 16]])),
      waits=[("cc", cc_done)], inc=("g_st", 16))
    stat_ld = 16
    V(lambda e: e.reduce_max(c_misc[0:1, 5:6],
                             bass.AP(rows, 2, [[max(4 * QB, QB2), 1], [16, GROUP]]),
                             axis=AX.X), waits=[("g_st", stat_ld)])
    V(lambda e: e.tensor_add(c_misc[0:1, 3:4], c_misc[0:1, 1:2],
                             c_misc[0:1, 5:6]))
    V(lambda e: e.tensor_scalar_mul(c_misc[0:1, 3:4], c_misc[0:1, 3:4],
                                    -0.5 * c.SCALE))
    V(lambda e: e.tensor_copy(rowbf[0:1, 0:1], c_misc[0:1, 3:4]), inc=("ve", 1))
    bias_ve = P.n("ve")
    T(lambda e: e.matmul(pb[6][:, 0:1], c_ones[0:1, 0:128], rowbf[0:1, 0:1],
                         start=True, stop=True),
      waits=[("ve", bias_ve)] + bw(6), inc=mp(6))
    V(lambda e: e.tensor_copy(biasT[:, :], pb[6][:, 0:1]), waits=br(6),
      inc=("ve", 1))
    freed(6, "ve")
    bias_ready = P.n("ve")

    kvq_ld = {}
    head_free_pe = {}

    def emit_kvq_load(h):
        extra = [head_free_pe[h - 2]] if h - 2 in head_free_pe else []
        for r in range(GROUP):
            def fk(e, h=h, r=r):
                return e.dma_start(
                    out=R2[:, R2_K + (h % 2) * c.SEQ + r * SL:
                           R2_K + (h % 2) * c.SEQ + (r + 1) * SL],
                    in_=k_out[r * D + h * 128: r * D + (h + 1) * 128, :])
            G(fk, waits=[("cc", cc_done)] + extra, inc=(f"g_kv{h % 2}", 16))
            def fv(e, h=h, r=r):
                src = bass.AP(v_out.tensor, (r * SL) * D + h * 128,
                              [[D, 128], [128 * D, SL // 128], [1, 128]])
                dst = R2[:, R2_V + (h % 2) * c.SEQ + r * SL:
                         R2_V + (h % 2) * c.SEQ + (r + 1) * SL] \
                    .rearrange("p (t e) -> p t e", t=SL // 128)
                return e.dma_start(out=dst, in_=src)
            G(fv, waits=[("cc", cc_done)] + extra, inc=(f"g_kv{h % 2}", 16))
        def fq(e, h=h):
            return e.dma_start(out=R3[:, R3_Q + (h % 2) * SL:
                                      R3_Q + (h % 2) * SL + SL],
                               in_=q_scr[h * 128:(h + 1) * 128, :])
        G(fq, waits=qkv_stores + extra, inc=(f"g_kv{h % 2}", 16))
        kvq_ld[h] = (f"g_kv{h % 2}", P.n(f"g_kv{h % 2}"))

    NH_EMIT = 0 if STOP_AFTER == "A" else NH
    if NH_EMIT:
        emit_kvq_load(0)
        if NH > 1:
            emit_kvq_load(1)

    as_free_st = [0] * c.NH2
    rs_free = []

    prev_av = {}
    prev_pair = {}
    for h in range(NH_EMIT):
        hb = h % 2
        for q2 in range(c.NQB2):
            av_pe = {}
            pair_ve = {}
            for kc in range(c.KCH):
                grp = (kc % 2) * 2
                pslot = kc % 4
                pguard = []
                if kc >= 4:
                    pguard.append(av_pe[kc - 4])
                    kpair = (kc - 4) if (kc - 4) % 2 == 1 else kc - 3
                    pguard.append(("ve", pair_ve[kpair]))
                elif prev_av:
                    ko = c.KCH - 4 + kc
                    pguard.append(prev_av[ko])
                    kpair = ko if ko % 2 == 1 else ko + 1
                    pguard.append(("ve", prev_pair[kpair]))
                for half in range(c.NH2):
                    def fs(e, kc=kc, half=half, grp=grp, hb=hb, q2=q2):
                        lhs = R2[:, R2_K + hb * c.SEQ + kc * 128:
                                 R2_K + hb * c.SEQ + (kc + 1) * 128]
                        rhs = R3[:, R3_Q + hb * SL + q2 * QB2 + half * 512:
                                 R3_Q + hb * SL + q2 * QB2 + half * 512 + 512]
                        return e.matmul(pb[grp + half][:, 0:512], lhs, rhs,
                                        start=True, stop=True)
                    T(fs, waits=[kvq_ld[h]] + bw(grp + half),
                      inc=mp(grp + half))
                def fx(e, grp=grp, pslot=pslot):
                    srcp = pA if grp == 0 else pB
                    return e.activation(
                        out=R3[:, R3_P + pslot * QB2:
                               R3_P + pslot * QB2 + QB2],
                        in_=srcp[:, 0:QB2], func=AF.Exp,
                        bias=biasT[:, :], scale=c.SCALE)
                S(fx, waits=[("ve", bias_ready)] + br(grp) + br(grp + c.NH2 - 1)
                  + pguard, inc=("se", 1))
                freed(grp, "se")
                freed(grp + c.NH2 - 1, "se")
                exp_se = P.n("se")
                for half in range(c.NH2):
                    def fa(e, kc=kc, half=half, hb=hb, pslot=pslot):
                        lhs = R2[:, R2_V + hb * c.SEQ + kc * 128:
                                 R2_V + hb * c.SEQ + (kc + 1) * 128]
                        rhs = R3[:, R3_P + pslot * QB2 + half * 512:
                                 R3_P + pslot * QB2 + half * 512 + 512]
                        return e.matmul(pb[4 + half][:, 0:512], lhs, rhs,
                                        start=(kc == 0), stop=(kc == c.KCH - 1))
                    T(fa, waits=[("se", exp_se)] + (bw(4 + half)
                                                    if kc == 0 else []),
                      inc=(mp(4 + half) if kc == c.KCH - 1
                           else (("pe", 1) if half == c.NH2 - 1 else None)))
                av_pe[kc] = (("pe", P.n("pe")) if kc != c.KCH - 1
                             else (f"bk{4 + c.NH2 - 1}",
                                   bank_prod[4 + c.NH2 - 1]))
                if kc % 2 == 1:
                    def fp(e, pslot=pslot):
                        a = R3[:, R3_P + (pslot - 1) * QB2:
                               R3_P + (pslot - 1) * QB2 + QB2]
                        b = R3[:, R3_P + pslot * QB2:R3_P + pslot * QB2 + QB2]
                        return e.tensor_add(R3[:, R3_PS:R3_PS + QB2], a, b)
                    V(fp, waits=[("se", exp_se)])
                    rguard = rs_free if kc == 1 else []
                    def fr(e, kc=kc):
                        if kc == 1:
                            return e.tensor_copy(rs32[:, :],
                                                 R3[:, R3_PS:R3_PS + QB2])
                        return e.tensor_add(rs32[:, :], rs32[:, :],
                                            R3[:, R3_PS:R3_PS + QB2])
                    V(fr, waits=rguard, inc=("ve", 1))
                    pair_ve[kc] = P.n("ve")
            rs_ve = P.n("ve")
            for half in range(c.NH2):
                def frs(e, half=half):
                    return e.matmul(pb[6 + half][0:1, 0:512], c_onecol32[:, :],
                                    rs32[:, half * 512:half * 512 + 512],
                                    start=True, stop=True)
                T(frs, waits=[("ve", rs_ve)] + bw(6 + half), inc=mp(6 + half))
            for half in range(c.NH2):
                def fre(e, half=half):
                    return e.tensor_copy(
                        rows[0:1, ROW_S1 + half * 512:ROW_S1 + half * 512 + 512],
                        pb[6 + half][0:1, 0:512])
                V(fre, waits=br(6 + half), inc=("ve", 1))
                freed(6 + half, "ve")
            V(lambda e: e.reciprocal(rows[0:1, ROW_S1:ROW_S1 + QB2],
                                     rows[0:1, ROW_S1:ROW_S1 + QB2]))
            V(lambda e: e.tensor_copy(rowbf[0:1, 0:QB2],
                                      rows[0:1, ROW_S1:ROW_S1 + QB2]),
              inc=("ve", 1))
            rr_ve = P.n("ve")
            rs_free = []
            for half in range(c.NH2):
                def fbc(e, half=half):
                    return e.matmul(pb[6 + half][:, 0:512], c_ones[0:1, 0:128],
                                    rowbf[0:1, half * 512:half * 512 + 512],
                                    start=True, stop=True)
                T(fbc, waits=[("ve", rr_ve)] + bw(6 + half), inc=mp(6 + half))
                rs_free.append((f"bk{6 + half}", bank_prod[6 + half]))
                def fbe(e, half=half):
                    return e.tensor_copy(rrs_bc[:, half * 512:half * 512 + 512],
                                         pb[6 + half][:, 0:512])
                V(fbe, waits=br(6 + half), inc=("ve", 1))
                freed(6 + half, "ve")
            rrsbc_ve = P.n("ve")
            for half in range(c.NH2):
                def fA(e, half=half):
                    return e.tensor_tensor(
                        R3[:, R3_AS + half * 512:R3_AS + half * 512 + 512],
                        pb[4 + half][:, 0:512],
                        rrs_bc[:, half * 512:half * 512 + 512], ALU.mult)
                V(fA, waits=[("ve", rrsbc_ve),
                             (f"g_as{half}", as_free_st[half])] +
                  br(4 + half), inc=("ve", 1))
                freed(4 + half, "ve")
                a_ve = P.n("ve")
                def fAs(e, half=half, h=h, q2=q2):
                    return e.dma_start(
                        out=a_scr[h * 128:(h + 1) * 128,
                                  q2 * QB2 + half * 512:
                                  q2 * QB2 + half * 512 + 512],
                        in_=R3[:, R3_AS + half * 512:R3_AS + half * 512 + 512])
                G(fAs, waits=[("ve", a_ve)], inc=(f"g_as{half}", 16))
                as_free_st[half] = P.n(f"g_as{half}")
            prev_av = av_pe
            prev_pair = pair_ve
        head_free_pe[h] = br(6 + c.NH2 - 1)[0]
        if h + 2 < NH:
            emit_kvq_load(h + 2)
    a_stores = [(f"g_as{i}", P.n(f"g_as{i}")) for i in range(c.NH2)]

    def fwo(e):
        src = bass.AP(wout.tensor, 0, [[D, 128], [128 * D, NH], [1, D]])
        dst = R2[:, 0:NH * D].rearrange("p (cc d) -> p cc d", cc=NH)
        return e.dma_start(out=dst, in_=src)
    G(fwo, waits=a_stores, inc=("g_wo", 16))

    prev_oproj_se = 0
    for qb in range(0 if STOP_AFTER in ("A", "B") else c.NQB):
        def fab(e, qb=qb):
            src = bass.AP(a_scr.tensor, qb * QB,
                          [[SL, 128], [128 * SL, NH], [1, QB]])
            dst = R3[:, R3_A:R3_A + NH * QB].rearrange(
                "p (cc t) -> p cc t", cc=NH)
            return e.dma_start(out=dst, in_=src)
        G(fab, waits=a_stores + [("se", prev_oproj_se)] +
          (preps_done if qb == 0 else []), inc=("g_al", 16))
        a_ld = P.n("g_al")
        o_se = []
        for dc in range(DC):
            bk = dc % 3
            for hc in range(NH):
                def fo(e, dc=dc, hc=hc, bk=bk):
                    lhs = R2[:, hc * D + dc * 128: hc * D + (dc + 1) * 128]
                    rhs = R3[:, R3_A + hc * QB:R3_A + (hc + 1) * QB]
                    return e.matmul(pb[bk][:, 0:QB], lhs, rhs,
                                    start=(hc == 0), stop=(hc == NH - 1))
                T(fo, waits=[("g_al", a_ld), ("g_wo", 16)] +
                  (bw(bk) if hc == 0 else []),
                  inc=(mp(bk) if hc == NH - 1 else None))
            def foe(e, dc=dc, bk=bk):
                return e.add(
                    out=R3[:, R3_O + dc * QB:R3_O + (dc + 1) * QB],
                    in_=pb[bk][:, 0:QB], add=c_bo2[:, dc:dc + 1])
            S(foe, waits=br(bk) + [("s_c", bias_cols)],
              inc=("se", 1))
            freed(bk, "se")
            o_se.append(P.n("se"))
        def osrc(dc):
            return R3[:, R3_O + dc * QB:R3_O + (dc + 1) * QB]
        bc_ve = emit_ln_stats(osrc, lambda dc: [("se", o_se[dc])], (4, 5), 6)
        for dc in range(DC):
            def f1(e, dc=dc, osrc=osrc):
                return e.tensor_tensor(stage[:, 0:QB], osrc(dc),
                                       bc_rstd[:, 0:QB], ALU.mult)
            V(f1, waits=[("ve", bc_ve)])
            def f2(e):
                return e.tensor_sub(stage[:, 0:QB], stage[:, 0:QB],
                                    bc_nu[:, 0:QB])
            V(f2)
            def f3(e, dc=dc, qb=qb):
                t = R1[:, dc * SL + qb * QB: dc * SL + qb * QB + QB]
                return e.tensor_add(t, t, stage[:, 0:QB])
            V(f3, inc=("ve", 1))
        x2_ve = P.n("ve")
        def fxs(e, qb=qb):
            srcap = R1[:, 0:DC * SL].rearrange(
                "p (cc t) -> p cc t", cc=DC)[:, :, qb * QB:(qb + 1) * QB]
            dst = bass.AP(x2_scr.tensor, qb * QB,
                          [[SL, 128], [128 * SL, DC], [1, QB]])
            return e.dma_start(out=dst, in_=srcap)
        G(fxs, waits=[("ve", x2_ve)], inc=("g_x2s", 16))
        prev_oproj_se = o_se[-1]
    x2_done = P.n("g_x2s")

    phaseC_guard = ("se", prev_oproj_se)
    span_free = [phaseC_guard, phaseC_guard]
    w2_free = [phaseC_guard] * 4
    tr_free = [0, 0]
    prev_ffn_ve = 0
    ffn_ln3_guard = []

    for qb in range(0 if STOP_AFTER in ("A", "B", "C") else c.NQB):
        def fx2(e, qb=qb):
            src = bass.AP(x2_scr.tensor, qb * QB,
                          [[SL, 128], [128 * SL, DC], [1, QB]])
            dst = R3[:, R3_X2:R3_X2 + DC * QB].rearrange(
                "p (cc t) -> p cc t", cc=DC)
            return e.dma_start(out=dst, in_=src)
        G(fx2, waits=[("g_x2s", x2_done), ("ve", prev_ffn_ve)],
          inc=("g_x2l", 16))
        x2_ld = P.n("g_x2l")
        def x2src(dc):
            return R3[:, R3_X2 + dc * QB:R3_X2 + (dc + 1) * QB]
        bc_ve = emit_ln_stats(x2src, lambda dc: [("g_x2l", x2_ld)], (4, 5), 6)
        for dc in range(DC):
            def fl(e, dc=dc, x2src=x2src):
                return e.tensor_tensor(
                    R3[:, R3_LN3 + dc * QB:R3_LN3 + (dc + 1) * QB],
                    x2src(dc), bc_rstd[:, 0:QB], ALU.mult)
            V(fl, waits=[("ve", bc_ve)] + ffn_ln3_guard)
            def fl2(e, dc=dc):
                return e.tensor_sub(
                    R3[:, R3_LN3 + dc * QB:R3_LN3 + (dc + 1) * QB],
                    R3[:, R3_LN3 + dc * QB:R3_LN3 + (dc + 1) * QB],
                    bc_nu[:, 0:QB])
            V(fl2, inc=("ve", 1))
        ln3_all = P.n("ve")
        if DEBUG_TAP:
            V(lambda e, qb=qb: DEBUG_TAP(e, f"d_ln3_qb{qb}"),
              waits=[("ve", ln3_all)])

        hid_free_ve = [0, 0]
        sg_free_ve = [0, 0]
        first_hida = True
        for sp in range(c.NSPAN):
            slot = sp % 2
            for wi, wt_scr in enumerate((w1t, w3t)):
                def fsl(e, sp=sp, wi=wi, slot=slot, wt_scr=wt_scr):
                    src = bass.AP(wt_scr.tensor, sp * c.SPAN * 128,
                                  [[HID, 128], [128 * HID, DC],
                                   [1, c.SPAN * 128]])
                    base = (wi * 2 + slot) * DC * c.SPAN * 128
                    dst = R2[:, base:base + DC * c.SPAN * 128] \
                        .rearrange("p (dc k) -> p dc k", dc=DC)
                    return e.dma_start(out=dst, in_=src)
                Y(fsl, waits=preps_done + [span_free[slot]],
                  inc=(f"s_sp{slot}", 16))
            ready = (f"s_sp{slot}", P.n(f"s_sp{slot}"))
            for hloc in range(c.SPAN):
                hc = sp * c.SPAN + hloc
                gb = hc % 2
                ub = 2 + hc % 2
                fbk = 4 + hc % 2
                for dc in range(DC):
                    def fg(e, dc=dc, hloc=hloc, slot=slot, gb=gb):
                        base = (0 * 2 + slot) * DC * c.SPAN * 128
                        lhs = R2[:, base + dc * c.SPAN * 128 + hloc * 128:
                                 base + dc * c.SPAN * 128 + (hloc + 1) * 128]
                        return e.matmul(pb[gb][:, 0:QB], lhs,
                                        R3[:, R3_LN3 + dc * QB:
                                           R3_LN3 + (dc + 1) * QB],
                                        start=(dc == 0), stop=(dc == DC - 1))
                    T(fg, waits=[("ve", ln3_all), ready] +
                      (bw(gb) if dc == 0 else []),
                      inc=(mp(gb) if dc == DC - 1 else None))
                for dc in range(DC):
                    def fu(e, dc=dc, hloc=hloc, slot=slot, ub=ub):
                        base = (1 * 2 + slot) * DC * c.SPAN * 128
                        lhs = R2[:, base + dc * c.SPAN * 128 + hloc * 128:
                                 base + dc * c.SPAN * 128 + (hloc + 1) * 128]
                        return e.matmul(pb[ub][:, 0:QB], lhs,
                                        R3[:, R3_LN3 + dc * QB:
                                           R3_LN3 + (dc + 1) * QB],
                                        start=(dc == 0), stop=(dc == DC - 1))
                    T(fu, waits=[ready] + (bw(ub) if dc == 0 else []),
                      inc=(mp(ub) if dc == DC - 1 else None))
                def fsl2(e, gb=gb, hc=hc):
                    return e.activation(
                        out=sg_st[:, (hc % 2) * QB:(hc % 2) * QB + QB],
                        in_=pb[gb][:, 0:QB], func=AF.Silu)
                S(fsl2, waits=br(gb) + [("ve", sg_free_ve[hc % 2])],
                  inc=("se", 1))
                freed(gb, "se")
                sl_se = P.n("se")
                hguard = [(f"bk{fbk}", bank_prod[fbk])]
                def fh(e, ub=ub, hc=hc):
                    return e.tensor_tensor(
                        hid_st[:, (hc % 2) * QB:(hc % 2) * QB + QB],
                        sg_st[:, (hc % 2) * QB:(hc % 2) * QB + QB],
                        pb[ub][:, 0:QB], ALU.mult)
                V(fh, waits=[("se", sl_se)] + br(ub) + hguard,
                  inc=("ve", 1))
                freed(ub, "ve")
                hid_ve = P.n("ve")
                sg_free_ve[hc % 2] = hid_ve
                def ff(e, fbk=fbk, hc=hc):
                    return e.matmul(pb[fbk][:, 0:QB], c_H128[:, :],
                                    hid_st[:, (hc % 2) * QB:(hc % 2) * QB + QB],
                                    start=True, stop=True)
                T(ff, waits=[("ve", hid_ve)] + bw(fbk), inc=mp(fbk))
                hguard2 = [("g_x2s", x2_done)] if first_hida else []
                first_hida = False
                def fe2(e, fbk=fbk, hc=hc):
                    return e.copy(out=R1[:, hc * QB:(hc + 1) * QB],
                                  in_=pb[fbk][:, 0:QB])
                S(fe2, waits=br(fbk) + hguard2, inc=("se", 1))
                freed(fbk, "se")
                if DEBUG_TAP and qb == 0 and hc == 0:
                    S(lambda e: DEBUG_TAP(e, "d_hc0"), waits=[("se", P.n("se"))])
            span_free[slot] = br(4 + (sp * c.SPAN + c.SPAN - 1) % 2)[0]
        hidA_se = P.n("se")
        if DEBUG_TAP:
            S(lambda e, qb=qb: DEBUG_TAP(e, f"d_hidA_qb{qb}"),
              waits=[("se", hidA_se)])

        o2_se = {}
        dc_groups = []
        i = 0
        while i < DC:
            dc_groups.append(list(range(i, min(i + 6, DC))))
            i += 6
        NW2S = 4
        for gi, grpd in enumerate(dc_groups):
            d0 = grpd[0]
            glen = len(grpd)
            for hc in range(HC):
                slot = hc % NW2S
                ssem = f"s_w2s{slot}"
                def fwl(e, hc=hc, slot=slot, d0=d0, glen=glen):
                    return e.dma_start(
                        out=w2s[:, slot * 768:slot * 768 + glen * 128],
                        in_=w2t[hc * 128:(hc + 1) * 128,
                                d0 * 128:(d0 + glen) * 128])
                Y(fwl, waits=preps_done + ([w2_free[slot]]
                                           if w2_free[slot] else []),
                  inc=(ssem, 16))
                wready = (ssem, P.n(ssem))
                for bi, dc in enumerate(grpd):
                    def fw2(e, hc=hc, bi=bi, slot=slot):
                        off = slot * 768
                        lhs = w2s[:, off + bi * 128:off + (bi + 1) * 128]
                        return e.matmul(pb[bi][:, 0:QB], lhs,
                                        R1[:, hc * QB:(hc + 1) * QB],
                                        start=(hc == 0), stop=(hc == HC - 1))
                    T(fw2, waits=[wready, ("se", hidA_se)] +
                      (bw(bi) if hc == 0 else []),
                      inc=(mp(bi) if hc == HC - 1 else
                           (("pe", 1) if bi == len(grpd) - 1 else None)))
                w2_free[slot] = (br(len(grpd) - 1)[0] if hc == HC - 1
                                 else ("pe", P.n("pe")))
            for bi, dc in enumerate(grpd):
                def fo2(e, dc=dc, bi=bi):
                    return e.copy(
                        out=R3[:, R3_LN3 + dc * QB:R3_LN3 + (dc + 1) * QB],
                        in_=pb[bi][:, 0:QB])
                S(fo2, waits=br(bi), inc=("se", 1))
                freed(bi, "se")
                o2_se[dc] = P.n("se")

        if DEBUG_TAP:
            S(lambda e, qb=qb: DEBUG_TAP(e, f"d_o2_qb{qb}"),
              waits=[("se", o2_se[DC - 1])])

        def o2src(dc):
            return R3[:, R3_LN3 + dc * QB:R3_LN3 + (dc + 1) * QB]
        bc_ve = emit_ln_stats(o2src, lambda dc: [("se", o2_se[dc])], (6, 7), 6)
        o32_free = [0, 0]
        for dc in range(DC):
            ob = dc % 2
            def f1(e, dc=dc, ob=ob, o2src=o2src):
                return e.tensor_tensor(o32[:, ob * QB:(ob + 1) * QB],
                                       o2src(dc), bc_rstd[:, 0:QB], ALU.mult)
            V(f1, waits=[("ve", bc_ve), ("ve", o32_free[ob])])
            def f2(e, ob=ob):
                return e.tensor_sub(o32[:, ob * QB:(ob + 1) * QB],
                                    o32[:, ob * QB:(ob + 1) * QB],
                                    bc_nu[:, 0:QB])
            V(f2)
            def f3(e, dc=dc, ob=ob, x2src=x2src):
                return e.tensor_add(o32[:, ob * QB:(ob + 1) * QB],
                                    o32[:, ob * QB:(ob + 1) * QB], x2src(dc))
            V(f3, inc=("ve", 1))
            of_ve = P.n("ve")
            for qt in range(c.QT):
                tb = 6 + (qt % 2)
                def ftr(e, dc=dc, qt=qt, ob=ob, tb=tb):
                    return e.transpose(pb[tb][:, 0:128],
                                       o32[:, ob * QB + qt * 128:
                                           ob * QB + (qt + 1) * 128],
                                       c_ident[:, :])
                T(ftr, waits=[("ve", of_ve)] + bw(tb), inc=mp(tb))
                def fte(e, qt=qt, tb=tb):
                    return e.tensor_copy(
                        tr_st[:, (qt % 2) * 128:(qt % 2) * 128 + 128],
                        pb[tb][:, 0:128])
                V(fte, waits=br(tb) + [(f"o_tr{qt % 2}", tr_free[qt % 2])],
                  inc=("ve", 1))
                freed(tb, "ve")
                te_ve = P.n("ve")
                def fto(e, dc=dc, qt=qt, qb=qb):
                    dst = bass.AP(out_hbm.tensor,
                                  (qb * QB + qt * 128) * D + dc * 128,
                                  [[D, 128], [1, 128]])
                    return e.dma_start(
                        out=dst, in_=tr_st[:, (qt % 2) * 128:
                                           (qt % 2) * 128 + 128])
                Y(fto, waits=[("ve", te_ve)], inc=(f"o_tr{qt % 2}", 16))
                tr_free[qt % 2] = P.n(f"o_tr{qt % 2}")
            o32_free[ob] = P.n("ve")
        prev_ffn_ve = P.n("ve")
        ffn_ln3_guard = [("se", o2_se[DC - 1])]

    Y(lambda e: None, waits=[("o_tr0", P.n("o_tr0")), ("o_tr1", P.n("o_tr1"))])

    P.check_deadlock()
    with contextlib.ExitStack() as ctx:
        sems = {s: ctx.enter_context(nc.semaphore(s)) for s in sem_names}
        block = ctx.enter_context(nc.Block())
        P.emit(block, sems)
    return nc



def _hadamard(n):
    h = np.array([[1.0]])
    while h.shape[0] < n:
        h = np.block([[h, h], [h, -h]])
    return h


def make_inputs(cfg, x, wqkv, bqkv, wout, bout, w1, w2, w3):
    Hi = (_hadamard(cfg.HI) / math.sqrt(cfg.HID)).astype(BF)
    H128 = _hadamard(128).astype(BF)
    ident = np.eye(128, dtype=np.float32)
    shared = dict(
        wqkv=np.ascontiguousarray(wqkv, dtype=np.float32),
        bqkv=np.ascontiguousarray(bqkv, dtype=np.float32).reshape(1, -1),
        wout=np.ascontiguousarray(wout, dtype=np.float32),
        bout=np.ascontiguousarray(bout, dtype=np.float32).reshape(1, -1),
        w1=np.ascontiguousarray(w1, dtype=np.float32),
        w2=np.ascontiguousarray(w2, dtype=np.float32),
        w3=np.ascontiguousarray(w3, dtype=np.float32),
        Hi_in=Hi, H128_in=H128, ident_in=ident,
    )
    in_maps = []
    for core in range(CORES):
        b = core // GROUP
        t = core % GROUP
        sl = x[b, t * cfg.SL:(t + 1) * cfg.SL, :]
        in_maps.append(dict(x_in=np.ascontiguousarray(sl, np.float32), **shared))
    return in_maps


def kernel(L0_attn_x, L0_attn_w_qkv, L0_attn_b_qkv, L0_attn_w_out, L0_attn_b_out,
           L0_ffn_W1, L0_ffn_W2, L0_ffn_W3):
    cfg = CFG_REAL
    x = np.asarray(L0_attn_x, dtype=np.float32)
    B, S, D = x.shape
    in_maps = make_inputs(cfg, x, L0_attn_w_qkv, L0_attn_b_qkv, L0_attn_w_out,
                          L0_attn_b_out, L0_ffn_W1, L0_ffn_W2, L0_ffn_W3)
    nc = build_program(cfg)
    res = run_bass_kernel_spmd(nc, in_maps, core_ids=list(range(CORES)))
    out = np.empty((B, S, D), np.float32)
    for core in range(CORES):
        b = core // GROUP
        t = core % GROUP
        out[b, t * cfg.SL:(t + 1) * cfg.SL, :] = res.results[core]["out"]
    return out
